# version 02
# baseline (speedup 1.0000x reference)
import numpy as np
import ml_dtypes
import concourse.bass as bass
import concourse.mybir as mybir
from concourse.bass_utils import run_bass_kernel_spmd
from contextlib import ExitStack

F32 = mybir.dt.float32
BF16 = mybir.dt.bfloat16
I8 = mybir.dt.int8
AX = mybir.AxisListType
ALU = mybir.AluOpType
ACT = mybir.ActivationFunctionType

SAFE_ROUND = False

S, D, H = 4096, 1024, 4096
T = 32
C = 8
HC = 32
LADDER = [64.0, 32.0, 16.0, 8.0, 4.0, 2.0, 1.0, 0.5]

ENGS = ("tensor", "vector", "scalar", "gpsimd", "sync")


class Prog:

    def __init__(self, sems):
        self.sems = sems
        self.count = {k: 0 for k in sems}
        self.items = {e: [] for e in ENGS}

    def emit(self, eng, fn, waits=(), inc=None, inc_n=1):
        w = tuple((s, v) for (s, v) in waits if v > 0)
        pt = None
        if inc is not None:
            self.count[inc] += inc_n
            pt = (inc, inc_n)
        self.items[eng].append((fn, w, pt))
        return (inc, self.count[inc]) if inc else None

    def now(self, sem):
        return (sem, self.count[sem])

    def replay(self, nc):
        items = self.items
        sems = self.sems
        with nc.Block() as block:
            def mk(name):
                def run(eng):
                    for fn, waits, pt in items[name]:
                        for (s, v) in waits:
                            eng.wait_ge(sems[s], v)
                        if fn is None:
                            continue
                        instr = fn(eng)
                        if pt is not None:
                            instr.then_inc(sems[pt[0]], pt[1])
                return run
            for name, dec in (("sync", block.sync), ("tensor", block.tensor),
                              ("vector", block.vector), ("scalar", block.scalar),
                              ("gpsimd", block.gpsimd)):
                if items[name]:
                    dec(mk(name))
        self.items = {e: [] for e in ENGS}


def build_nc(safe, phases=4, p2_tiles=T, noacc=False):
    nc = bass.Bass("TRN2", target_bir_lowering=False)

    x_d = nc.dram_tensor("x", [S, D], F32, kind="ExternalInput").ap()
    wq_d = nc.dram_tensor("wq", [D, D], F32, kind="ExternalInput").ap()
    wk_d = nc.dram_tensor("wk", [D, D], F32, kind="ExternalInput").ap()
    wv_d = nc.dram_tensor("wv", [D, D], F32, kind="ExternalInput").ap()
    w1_d = nc.dram_tensor("w1", [H, D], F32, kind="ExternalInput").ap()
    cw_d = nc.dram_tensor("cw", [H, 3], F32, kind="ExternalInput").ap()
    cb_d = nc.dram_tensor("cb", [H], F32, kind="ExternalInput").ap()
    w2_d = nc.dram_tensor("w2", [D, H], F32, kind="ExternalInput").ap()
    out_d = nc.dram_tensor("out", [S, D], F32, kind="ExternalOutput").ap()

    ixT_d = nc.dram_tensor("ixT_s", [D, S + 2], BF16, kind="Internal").ap()
    qT_d = nc.dram_tensor("qT_s", [D, S], BF16, kind="Internal").ap()
    kT2_d = nc.dram_tensor("kT2_s", [D, S // 2], BF16, kind="Internal").ap()
    v2_d = nc.dram_tensor("v2_s", [S // 2, D], BF16, kind="Internal").ap()
    attn_d = nc.dram_tensor("attn_s", [S, D], BF16, kind="Internal").ap()
    row1_d = nc.dram_tensor("row1_s", [1, S], F32, kind="Internal").ap()

    st = ExitStack()
    e = st.enter_context
    semnames = (["t", "v", "s", "g", "pad", "kv", "cwb", "sx", "ror"]
                + [f"x{k}" for k in range(3)]
                + [f"w{k}" for k in range(2)]
                + [f"q{k}" for k in range(2)]
                + [f"st{k}" for k in range(4)]
                + [f"ix{k}" for k in range(2)]
                + [f"as{k}" for k in range(2)]
                + [f"iw{k}" for k in range(2)]
                + [f"xr{k}" for k in range(2)]
                + [f"ar{k}" for k in range(2)]
                + [f"ou{k}" for k in range(2)]
                + [f"wf{k}" for k in range(2)])
    sems = {n: e(nc.semaphore(n)) for n in semnames}
    P = Prog(sems)
    STORE_SEMS = ["pad"] + [f"st{k}" for k in range(4)] + [f"ix{k}" for k in range(2)] \
                 + [f"as{k}" for k in range(2)] + [f"ou{k}" for k in range(2)] + ["ror"]
    def all_stores():
        return [P.now(s) for s in STORE_SEMS]

    iot = e(nc.sbuf_tensor("iot", [128, 128], F32))
    identf = e(nc.sbuf_tensor("identf", [128, 128], F32))
    identb = e(nc.sbuf_tensor("identb", [128, 128], BF16))
    caumask = e(nc.sbuf_tensor("caumask", [128, 128], F32))
    halfcol = e(nc.sbuf_tensor("halfcol", [128, 1], F32))
    epscol = e(nc.sbuf_tensor("epscol", [128, 1], F32))
    onesrow = e(nc.sbuf_tensor("onesrow", [1, 128], F32))
    zpad = e(nc.sbuf_tensor("zpad", [128, 8], BF16))
    stat_mx = e(nc.sbuf_tensor("stat_mx", [128, T], F32))
    colA = e(nc.sbuf_tensor("colA", [128, 1], F32))
    colB = e(nc.sbuf_tensor("colB", [128, 1], F32))
    colC = e(nc.sbuf_tensor("colC", [128, 1], F32))
    colD = e(nc.sbuf_tensor("colD", [128, 1], F32))
    w2mc = e(nc.sbuf_tensor("w2mc", [128, 1], F32))
    srow1 = e(nc.sbuf_tensor("srow1", [1, 1], F32))
    srow2 = e(nc.sbuf_tensor("srow2", [1, 1], F32))
    srow3 = e(nc.sbuf_tensor("srow3", [1, 1], F32))

    P.emit("gpsimd", lambda g: g.iota(iot[:], pattern=[[1, 128]], base=0,
                                      channel_multiplier=-1,
                                      allow_small_or_imprecise_dtypes=True))
    P.emit("gpsimd", lambda g: g.memset(halfcol[:], 0.5))
    P.emit("gpsimd", lambda g: g.memset(epscol[:], 1e-5))
    P.emit("gpsimd", lambda g: g.memset(onesrow[:], 1.0))
    P.emit("gpsimd", lambda g: g.memset(zpad[:], 0.0), inc="g")
    gdone = P.now("g")
    P.emit("vector", lambda v: v.tensor_scalar(out=identf[:], in0=iot[:], scalar1=0.0,
                                               scalar2=None, op0=ALU.is_equal),
           waits=[gdone], inc="v")
    P.emit("vector", lambda v: v.tensor_copy(out=identb[:], in_=identf[:]),
           waits=[P.now("v")], inc="v")
    P.emit("vector", lambda v: v.tensor_scalar(out=caumask[:], in0=iot[:], scalar1=1.0,
                                               scalar2=-1.0e30, op0=ALU.is_ge, op1=ALU.mult),
           inc="v")

    def _pad(off):
        def f(sy):
            return sy.dma_start(
                out=bass.AP(ixT_d.tensor, off, [[S + 2, 128], [128 * (S + 2), 8], [1, 1]]),
                in_=bass.AP(zpad[:].tensor, 0, [[8, 128], [1, 8], [1, 1]]),
                allow_slow_non_contiguous=True)
        return f
    P.emit("sync", _pad(0), waits=[gdone], inc="pad", inc_n=16)
    P.emit("sync", _pad(S + 1), inc="pad", inc_n=16)
    P.replay(nc)

    with ExitStack() as stA:
        eA = stA.enter_context
        kt1 = eA(nc.sbuf_tensor("kt1", [128, C * 2048], BF16))
        vres = eA(nc.sbuf_tensor("vres", [128, 16 * 1024], BF16))

        with ExitStack() as stB:
            eB = stB.enter_context
            xnT = eB(nc.sbuf_tensor("xnT", [128, C * S], BF16))
            xin = [eB(nc.sbuf_tensor(f"xin{k}", [128, D], F32)) for k in range(3)]
            csc = eB(nc.sbuf_tensor("csc", [128, D], F32))
            sqs = eB(nc.sbuf_tensor("sqs", [128, D], F32))
            xnb = eB(nc.sbuf_tensor("xnb", [128, D], BF16))
            ixb = eB(nc.sbuf_tensor("ixb", [128, D], BF16))
            if safe:
                lu = eB(nc.sbuf_tensor("lu", [128, D], F32))
                lb = eB(nc.sbuf_tensor("lb", [128, D], BF16))
                ix8 = None
            else:
                ix8 = eB(nc.sbuf_tensor("ix8", [128, D], I8))
                lu = lb = None
            ixstg = [eB(nc.sbuf_tensor(f"ixstg{k}", [128, C * 128], BF16)) for k in range(2)]
            wf32 = [eB(nc.sbuf_tensor(f"wf32{k}", [128, D], F32)) for k in range(2)]
            wb = eB(nc.sbuf_tensor("wb", [128, C * D], BF16))
            spst = [eB(nc.sbuf_tensor(f"spst{k}", [128, 512], BF16)) for k in range(4)]
            pT = [eB(nc.psum_tensor(f"pT{k}", [128, 512], BF16)) for k in range(2)]
            pj = [eB(nc.psum_tensor(f"pj{k}", [128, 512], F32)) for k in range(4)]

            xin_guard = [None, None, None]
            ixstg_guard = [None, None]
            spst_guard = [None, None, None, None]
            pT_guard = [None, None]
            pj_guard = [None, None, None, None]

            for i in range(T if phases >= 2 else 0):
                sl = i % 3
                w = [xin_guard[sl]] if xin_guard[sl] else []
                P.emit("gpsimd", (lambda sl_, i_: lambda g: g.dma_start(
                    out=xin[sl_][:], in_=x_d[128 * i_:128 * (i_ + 1), :]))(sl, i),
                    waits=w, inc=f"x{sl}", inc_n=16)
                ld = P.now(f"x{sl}")
                xa = xin[sl]

                P.emit("vector", (lambda xa_: lambda v: v.reduce_sum(
                    out=colA[:], in_=xa_[:], axis=AX.X))(xa), waits=[ld], inc="v")
                P.emit("vector", lambda v: v.tensor_scalar(
                    out=colA[:], in0=colA[:], scalar1=1.0 / D, scalar2=None, op0=ALU.mult),
                    waits=[P.now("v")], inc="v")
                P.emit("vector", (lambda xa_: lambda v: v.tensor_scalar(
                    out=csc[:], in0=xa_[:], scalar1=colA[:], scalar2=None, op0=ALU.subtract))(xa),
                    waits=[P.now("v")], inc="v")
                cdone = P.now("v")
                xin_guard[sl] = cdone
                P.emit("scalar", lambda s: s.activation(out=sqs[:], in_=csc[:], func=ACT.Square,
                                                        bias=0.0, scale=1.0),
                       waits=[cdone], inc="s")
                P.emit("vector", lambda v: v.reduce_sum(out=colB[:], in_=sqs[:], axis=AX.X),
                       waits=[P.now("s")], inc="v")
                P.emit("scalar", lambda s: s.activation(out=colB[:], in_=colB[:], func=ACT.Sqrt,
                                                        bias=epscol[:], scale=1.0 / D),
                       waits=[P.now("v")], inc="s")
                P.emit("vector", lambda v: v.reciprocal(out=colB[:], in_=colB[:]),
                       waits=[P.now("s")], inc="v")
                rstd_done = P.now("v")
                P.emit("vector", lambda v: v.reduce_max(out=colC[:], in_=csc[:], axis=AX.X,
                                                        apply_absolute_value=True),
                       waits=[cdone], inc="v")
                P.emit("vector", lambda v: v.tensor_scalar(out=colC[:], in0=colC[:], scalar1=colB[:],
                                                           scalar2=1e-5, op0=ALU.mult, op1=ALU.max),
                       waits=[P.now("v"), rstd_done], inc="v")
                P.emit("vector", lambda v: v.reciprocal(out=colD[:], in_=colC[:]),
                       waits=[P.now("v")], inc="v")
                P.emit("vector", lambda v: v.tensor_scalar(out=colD[:], in0=colD[:], scalar1=127.0,
                                                           scalar2=None, op0=ALU.mult),
                       waits=[P.now("v")], inc="v")
                P.emit("vector", (lambda i_: lambda v: v.tensor_copy(
                    out=stat_mx[:, i_:i_ + 1], in_=colC[:]))(i), waits=[P.now("v")], inc="v")
                P.emit("vector", lambda v: v.tensor_scalar(out=colD[:], in0=colD[:], scalar1=colB[:],
                                                           scalar2=None, op0=ALU.mult),
                       waits=[P.now("v")], inc="v")
                psc_done = P.now("v")
                P.emit("vector", lambda v: v.tensor_scalar(out=xnb[:], in0=csc[:], scalar1=colB[:],
                                                           scalar2=None, op0=ALU.mult),
                       waits=[psc_done], inc="v")
                xnb_done = P.now("v")
                if not safe:
                    P.emit("vector", lambda v: v.tensor_scalar(out=ix8[:], in0=csc[:], scalar1=colD[:],
                                                               scalar2=None, op0=ALU.mult),
                           waits=[psc_done], inc="v")
                    P.emit("vector", lambda v: v.tensor_copy(out=ixb[:], in_=ix8[:]),
                           waits=[P.now("v")], inc="v")
                else:
                    P.emit("vector", lambda v: v.tensor_scalar(out=sqs[:], in0=csc[:], scalar1=colD[:],
                                                               scalar2=None, op0=ALU.mult),
                           waits=[psc_done], inc="v")
                    P.emit("scalar", lambda s: s.activation(out=lb[:], in_=sqs[:], func=ACT.Sign,
                                                            bias=halfcol[:], scale=1.0),
                           waits=[P.now("v")], inc="s")
                    for kk, cc in enumerate(LADDER):
                        src = sqs if kk == 0 else lu
                        P.emit("vector", (lambda cc_, src_: lambda v: v.scalar_tensor_tensor(
                            out=lu[:], in0=lb[:], scalar=-cc_, in1=src_[:],
                            op0=ALU.mult, op1=ALU.add))(cc, src),
                            waits=[P.now("s")], inc="v")
                        P.emit("scalar", (lambda last: lambda s: s.activation(
                            out=lb[:], in_=lu[:], func=ACT.Sign,
                            bias=(0.0 if last else halfcol[:]), scale=1.0))(kk == 7),
                            waits=[P.now("v")], inc="s")
                    P.emit("vector", lambda v: v.scalar_tensor_tensor(
                        out=lu[:], in0=lb[:], scalar=-0.5, in1=lu[:], op0=ALU.mult, op1=ALU.add),
                        waits=[P.now("s")], inc="v")
                    P.emit("vector", lambda v: v.tensor_tensor(out=ixb[:], in0=lu[:], in1=sqs[:],
                                                               op=ALU.subtract),
                           waits=[P.now("v")], inc="v")
                ix_done = P.now("v")

                for src_t, is_xn in ((xnb, True), (ixb, False)):
                    src_done = xnb_done if is_xn else ix_done
                    ssl = i % 2
                    for half in range(2):
                        pg = half
                        ptile = pT[pg]
                        tw = [src_done] + ([pT_guard[pg]] if pT_guard[pg] else [])
                        for q in range(4):
                            cch = 4 * half + q
                            P.emit("tensor", (lambda c_, q_, p_, s_: lambda t: t.transpose(
                                p_[:, 128 * q_:128 * (q_ + 1)], s_[:, 128 * c_:128 * (c_ + 1)],
                                identb[:]))(cch, q, ptile, src_t),
                                waits=tw if q == 0 else (), inc="t" if q == 3 else None)
                        tdone = P.now("t")
                        if is_xn:
                            def mkev(i_, h_, p_):
                                def f(v):
                                    dst = bass.AP(xnT[:].tensor, (4 * h_) * S + 128 * i_,
                                                  [[C * S, 128], [S, 4], [1, 128]])
                                    src = bass.AP(p_[:].tensor, 0, [[512, 128], [128, 4], [1, 128]])
                                    return v.tensor_copy(out=dst, in_=src)
                                return f
                            P.emit("vector", mkev(i, half, ptile), waits=[tdone], inc="v")
                        else:
                            def mkev2(sl_, h_, p_):
                                def f(v):
                                    dst = bass.AP(ixstg[sl_][:].tensor, 128 * 4 * h_,
                                                  [[C * 128, 128], [128, 4], [1, 128]])
                                    src = bass.AP(p_[:].tensor, 0, [[512, 128], [128, 4], [1, 128]])
                                    return v.tensor_copy(out=dst, in_=src)
                                return f
                            evw = [tdone]
                            if half == 0 and ixstg_guard[ssl]:
                                evw.append(ixstg_guard[ssl])
                            P.emit("vector", mkev2(ssl, half, ptile), waits=evw, inc="v")
                        pT_guard[pg] = P.now("v")
                    if not is_xn:
                        def mkd(sl_, i_):
                            def f(sy):
                                dst = bass.AP(ixT_d.tensor, 1 + 128 * i_,
                                              [[S + 2, 128], [128 * (S + 2), C], [1, 128]])
                                src = bass.AP(ixstg[sl_][:].tensor, 0,
                                              [[C * 128, 128], [128, C], [1, 128]])
                                return sy.dma_start(out=dst, in_=src)
                            return f
                        P.emit("sync", mkd(ssl, i), waits=[P.now("v")], inc=f"ix{ssl}", inc_n=16)
                        ixstg_guard[ssl] = P.now(f"ix{ssl}")

            wf32_guard = [None, None]

            def load_weight(dram, extra_wait):
                for cch in range(C):
                    wsl = cch % 2
                    w = [wf32_guard[wsl]] if wf32_guard[wsl] else []
                    P.emit("gpsimd", (lambda c_, w_: lambda g: g.dma_start(
                        out=wf32[w_][:], in_=dram[128 * c_:128 * (c_ + 1), :]))(cch, wsl),
                        waits=w, inc=f"w{wsl}", inc_n=16)
                    cw_ = [P.now(f"w{wsl}")] + (extra_wait if cch == 0 else [])
                    P.emit("vector", (lambda c_, w_: lambda v: v.tensor_copy(
                        out=wb[:, D * c_:D * (c_ + 1)], in_=wf32[w_][:]))(cch, wsl),
                        waits=cw_, inc="v")
                    wf32_guard[wsl] = P.now("v")
                return P.now("v")

            def proj_kq(ready, resident, spill_dram, scale):
                for dout in range(C):
                    for tg in range(8):
                        pslot = (dout * 8 + tg) % 4
                        ps = pj[pslot]
                        mmw = [ready] + ([pj_guard[pslot]] if pj_guard[pslot] else [])
                        for din in range(C):
                            P.emit("tensor", (lambda do_, tg_, di_, ps_: lambda t: t.matmul(
                                ps_[:], wb[:, D * di_ + 128 * do_:D * di_ + 128 * (do_ + 1)],
                                xnT[:, S * di_ + 512 * tg_:S * di_ + 512 * (tg_ + 1)],
                                start=(di_ == 0), stop=(di_ == C - 1)))(dout, tg, din, ps),
                                waits=mmw if din == 0 else (), inc="t" if din == C - 1 else None)
                        mmdone = P.now("t")
                        if resident is not None and tg < 4:
                            P.emit("scalar", (lambda do_, tg_, ps_: lambda s: s.activation(
                                out=kt1[:, 2048 * do_ + 512 * tg_:2048 * do_ + 512 * (tg_ + 1)],
                                in_=ps_[:], func=ACT.Copy, bias=0.0, scale=1.0))(dout, tg, ps),
                                waits=[mmdone], inc="s")
                            pj_guard[pslot] = P.now("s")
                        else:
                            ssl2 = (dout * 8 + tg) % 4
                            sw = [mmdone] + ([spst_guard[ssl2]] if spst_guard[ssl2] else [])
                            P.emit("scalar", (lambda ps_, sl_, sc_: lambda s: s.activation(
                                out=spst[sl_][:], in_=ps_[:], func=ACT.Copy, bias=0.0,
                                scale=sc_))(ps, ssl2, scale),
                                waits=sw, inc="s")
                            pj_guard[pslot] = P.now("s")
                            col0 = 512 * (tg - 4) if resident is not None else 512 * tg
                            P.emit("sync", (lambda do_, c0_, sl_, dr_: lambda sy: sy.dma_start(
                                out=dr_[128 * do_:128 * (do_ + 1), c0_:c0_ + 512],
                                in_=spst[sl_][:]))(dout, col0, ssl2, spill_dram),
                                waits=[P.now("s")], inc=f"st{ssl2}", inc_n=16)
                            spst_guard[ssl2] = P.now(f"st{ssl2}")
                return P.now("t")

            if phases < 2:
                P.replay(nc)
                return nc
            wk_ready = load_weight(wk_d, [])
            k_done = proj_kq(wk_ready, kt1, kT2_d, 1.0)

            wv_ready = load_weight(wv_d, [k_done])
            for tb in range(T):
                for hf in range(2):
                    pslot = (tb * 2 + hf) % 4
                    ps = pj[pslot]
                    mmw = [wv_ready] + ([pj_guard[pslot]] if pj_guard[pslot] else [])
                    for din in range(C):
                        P.emit("tensor", (lambda tb_, hf_, di_, ps_: lambda t: t.matmul(
                            ps_[:], xnT[:, S * di_ + 128 * tb_:S * di_ + 128 * (tb_ + 1)],
                            wb[:, D * di_ + 512 * hf_:D * di_ + 512 * (hf_ + 1)],
                            start=(di_ == 0), stop=(di_ == C - 1)))(tb, hf, din, ps),
                            waits=mmw if din == 0 else (), inc="t" if din == C - 1 else None)
                    mmdone = P.now("t")
                    if tb < 16:
                        P.emit("scalar", (lambda tb_, hf_, ps_: lambda s: s.activation(
                            out=vres[:, 1024 * tb_ + 512 * hf_:1024 * tb_ + 512 * (hf_ + 1)],
                            in_=ps_[:], func=ACT.Copy, bias=0.0, scale=1.0))(tb, hf, ps),
                            waits=[mmdone], inc="s")
                        pj_guard[pslot] = P.now("s")
                    else:
                        ssl2 = (tb * 2 + hf) % 4
                        sw = [mmdone] + ([spst_guard[ssl2]] if spst_guard[ssl2] else [])
                        P.emit("scalar", (lambda ps_, sl_: lambda s: s.activation(
                            out=spst[sl_][:], in_=ps_[:], func=ACT.Copy, bias=0.0, scale=1.0))(ps, ssl2),
                            waits=sw, inc="s")
                        pj_guard[pslot] = P.now("s")
                        P.emit("sync", (lambda tb_, hf_, sl_: lambda sy: sy.dma_start(
                            out=v2_d[128 * (tb_ - 16):128 * (tb_ - 15), 512 * hf_:512 * (hf_ + 1)],
                            in_=spst[sl_][:]))(tb, hf, ssl2),
                            waits=[P.now("s")], inc=f"st{ssl2}", inc_n=16)
                        spst_guard[ssl2] = P.now(f"st{ssl2}")
            v_done = P.now("t")

            wq_ready = load_weight(wq_d, [v_done])
            proj_kq(wq_ready, None, qT_d, 1.0 / 32.0)

            P.replay(nc)

        with ExitStack() as stC:
            eC = stC.enter_context
            kt2 = eC(nc.sbuf_tensor("kt2", [128, C * 2048], BF16))
            v2s = eC(nc.sbuf_tensor("v2s", [128, 16 * 1024], BF16))
            stile = eC(nc.sbuf_tensor("stile", [128, S], F32))
            ptile = eC(nc.sbuf_tensor("ptile", [128, S], BF16))
            pTt = eC(nc.sbuf_tensor("pTt", [128, S], BF16))
            qbuf = [eC(nc.sbuf_tensor(f"qbuf{k}", [128, C * 128], BF16)) for k in range(2)]
            astg = [eC(nc.sbuf_tensor(f"astg{k}", [128, D], BF16)) for k in range(2)]
            mcol = eC(nc.sbuf_tensor("mcol", [128, 1], F32))
            lcol = eC(nc.sbuf_tensor("lcol", [128, 1], F32))
            rcol = eC(nc.sbuf_tensor("rcol", [128, 1], F32))
            psS = [eC(nc.psum_tensor(f"psS{k}", [128, 512], F32)) for k in range(2)]
            psT2 = eC(nc.psum_tensor("psT2", [128, 512], BF16))
            psPV = eC(nc.psum_tensor("psPV", [128, 1024], F32))

            qbuf_guard = [None, None]
            astg_guard = [None, None]
            psS_guard = [[], []]
            psT2_guard = None
            psPV_guard = None

            def qload(i):
                sl = i % 2
                w = [qbuf_guard[sl]] if qbuf_guard[sl] else []
                P.emit("gpsimd", (lambda i_, sl_: lambda g: g.dma_start(
                    out=qbuf[sl_][:], in_=bass.AP(qT_d.tensor, 128 * i_,
                                                  [[S, 128], [128 * S, C], [1, 128]])))(i, sl),
                    waits=w, inc=f"q{sl}", inc_n=16)
                return P.now(f"q{sl}")

            qready = [None] * T
            P.emit("gpsimd", None, waits=all_stores())
            if phases < 3:
                P.replay(nc)
                return nc
            qready[0] = qload(0)
            if p2_tiles > 1:
                qready[1] = qload(1)
            P.emit("gpsimd", lambda g: g.dma_start(
                out=kt2[:], in_=bass.AP(kT2_d.tensor, 0, [[2048, 128], [128 * 2048, C], [1, 2048]])),
                inc="kv", inc_n=16)
            P.emit("gpsimd", lambda g: g.dma_start(
                out=v2s[:], in_=bass.AP(v2_d.tensor, 0, [[1024, 128], [128 * 1024, 16], [1, 1024]])),
                inc="kv", inc_n=16)
            kv2_ready = P.now("kv")

            for i in range(p2_tiles):
                sl = i % 2
                ncols = 128 * (i + 1)
                ngroups = (ncols + 511) // 512
                sc_done = []
                for gidx in range(ngroups):
                    g0 = 512 * gidx
                    gw = min(512, ncols - g0)
                    pslot = gidx % 2
                    ps = psS[pslot]
                    mmw = [qready[i]] + psS_guard[pslot]
                    if g0 + gw > 2048:
                        mmw.append(kv2_ready)
                    for din in range(C):
                        if g0 < 2048:
                            rhs = kt1[:, 2048 * din + g0:2048 * din + g0 + gw]
                        else:
                            rhs = kt2[:, 2048 * din + (g0 - 2048):2048 * din + (g0 - 2048) + gw]
                        P.emit("tensor", (lambda sl_, di_, ps_, rhs_, gw_: lambda t: t.matmul(
                            ps_[:, 0:gw_], qbuf[sl_][:, 128 * di_:128 * (di_ + 1)], rhs_,
                            start=(di_ == 0), stop=(di_ == C - 1)))(sl, din, ps, rhs, gw),
                            waits=mmw if din == 0 else (), inc="t" if din == C - 1 else None)
                    mmdone = P.now("t")
                    newguard = []
                    ndiag = gw - 128 if gidx == ngroups - 1 else gw
                    if ndiag > 0:
                        P.emit("scalar", (lambda ps_, g0_, nd_: lambda s: s.activation(
                            out=stile[:, g0_:g0_ + nd_], in_=ps_[:, 0:nd_], func=ACT.Copy,
                            bias=0.0, scale=1.0))(ps, g0, ndiag),
                            waits=[mmdone], inc="s")
                        newguard.append(P.now("s"))
                        sc_done.append(P.now("s"))
                    if gidx == ngroups - 1:
                        vw = [mmdone] + ([P.now("s")] if ndiag > 0 else [])
                        P.emit("vector", (lambda ps_, g0_, nd_: lambda v: v.tensor_tensor(
                            out=stile[:, g0_ + nd_:g0_ + nd_ + 128], in0=ps_[:, nd_:nd_ + 128],
                            in1=caumask[:], op=ALU.add))(ps, g0, ndiag),
                            waits=vw, inc="v")
                        newguard.append(P.now("v"))
                        sc_done.append(P.now("v"))
                    psS_guard[pslot] = newguard
                qbuf_guard[sl] = P.now("t")
                if i + 2 < p2_tiles:
                    qready[i + 2] = qload(i + 2)

                P.emit("vector", (lambda nc_: lambda v: v.reduce_max(
                    out=mcol[:], in_=stile[:, 0:nc_], axis=AX.X))(ncols),
                    waits=sc_done, inc="v")
                P.emit("vector", lambda v: v.tensor_scalar(out=mcol[:], in0=mcol[:], scalar1=-1.0,
                                                           scalar2=None, op0=ALU.mult),
                       waits=[P.now("v")], inc="v")
                if noacc:
                    P.emit("scalar", (lambda nc_: lambda s: s.activation(
                        out=ptile[:, 0:nc_], in_=stile[:, 0:nc_], func=ACT.Exp,
                        bias=mcol[:], scale=1.0))(ncols),
                        waits=[P.now("v")], inc="s")
                    exp_done = P.now("s")
                    P.emit("vector", (lambda nc_: lambda v: v.reduce_sum(
                        out=lcol[:], in_=ptile[:, 0:nc_], axis=AX.X))(ncols),
                        waits=[exp_done, P.now("v")], inc="v")
                else:
                    P.emit("scalar", (lambda nc_: lambda s: s.activation(
                        out=ptile[:, 0:nc_], in_=stile[:, 0:nc_], func=ACT.Exp,
                        bias=mcol[:], scale=1.0, accum_out=lcol[:]))(ncols),
                        waits=[P.now("v")], inc="s")
                    exp_done = P.now("s")
                P.emit("vector", lambda v: v.reciprocal(out=rcol[:], in_=lcol[:]),
                       waits=[exp_done, P.now("v")], inc="v")
                rdone = P.now("v")

                nkb = i + 1
                last_pt = None
                for b0 in range(0, nkb, 4):
                    bn = min(4, nkb - b0)
                    tw = [exp_done] + ([psT2_guard] if psT2_guard else [])
                    for q in range(bn):
                        kb = b0 + q
                        P.emit("tensor", (lambda kb_, q_: lambda t: t.transpose(
                            psT2[:, 128 * q_:128 * (q_ + 1)], ptile[:, 128 * kb_:128 * (kb_ + 1)],
                            identb[:]))(kb, q),
                            waits=tw if q == 0 else (), inc="t" if q == bn - 1 else None)
                    tdone = P.now("t")
                    P.emit("vector", (lambda b0_, bn_: lambda v: v.tensor_copy(
                        out=pTt[:, 128 * b0_:128 * (b0_ + bn_)], in_=psT2[:, 0:128 * bn_]))(b0, bn),
                        waits=[tdone], inc="v")
                    psT2_guard = P.now("v")
                    last_pt = psT2_guard

                mmw = [last_pt] + ([psPV_guard] if psPV_guard else [])
                if i >= 16:
                    mmw.append(kv2_ready)
                for kb in range(nkb):
                    vsrc = vres if kb < 16 else v2s
                    voff = 1024 * kb if kb < 16 else 1024 * (kb - 16)
                    for hf in range(2):
                        P.emit("tensor", (lambda kb_, hf_, vs_, vo_, nk_: lambda t: t.matmul(
                            psPV[:, 512 * hf_:512 * (hf_ + 1)],
                            pTt[:, 128 * kb_:128 * (kb_ + 1)],
                            vs_[:, vo_ + 512 * hf_:vo_ + 512 * (hf_ + 1)],
                            start=(kb_ == 0), stop=(kb_ == nk_ - 1)))(kb, hf, vsrc, voff, nkb),
                            waits=mmw if (kb == 0 and hf == 0) else (),
                            inc="t" if (kb == nkb - 1 and hf == 1) else None)
                pv_done = P.now("t")
                asl = i % 2
                ew = [pv_done, rdone] + ([astg_guard[asl]] if astg_guard[asl] else [])
                P.emit("vector", (lambda asl_: lambda v: v.tensor_scalar(
                    out=astg[asl_][:], in0=psPV[:], scalar1=rcol[:], scalar2=None,
                    op0=ALU.mult))(asl),
                    waits=ew, inc="v")
                psPV_guard = P.now("v")
                P.emit("sync", (lambda i_, asl_: lambda sy: sy.dma_start(
                    out=attn_d[128 * i_:128 * (i_ + 1), :], in_=astg[asl_][:]))(i, asl),
                    waits=[P.now("v")], inc=f"as{asl}", inc_n=16)
                astg_guard[asl] = P.now(f"as{asl}")

            P.replay(nc)

    with ExitStack() as stD:
        eD = stD.enter_context
        w1qT = eD(nc.sbuf_tensor("w1qT", [128, C * H], BF16))
        w2qT = eD(nc.sbuf_tensor("w2qT", [128, HC * 1024], BF16))
        trow = eD(nc.sbuf_tensor("trow", [32, 128], F32))
        h2 = eD(nc.sbuf_tensor("h2", [128, HC * 132], F32))
        ihb = eD(nc.sbuf_tensor("ihb", [128, HC * 128], BF16))
        mnsb = ihb
        scr16 = eD(nc.sbuf_tensor("scr16", [128, HC * 128], F32))
        ixw = [eD(nc.sbuf_tensor(f"ixw{k}", [128, C * 132], BF16)) for k in range(2)]
        xres = [eD(nc.sbuf_tensor(f"xres{k}", [128, D], F32)) for k in range(2)]
        ares = [eD(nc.sbuf_tensor(f"ares{k}", [128, D], BF16)) for k in range(2)]
        cwt = eD(nc.sbuf_tensor("cwt", [128, 3 * HC], F32))
        cbt = eD(nc.sbuf_tensor("cbt", [128, HC], F32))
        q0s = eD(nc.sbuf_tensor("q0s", [128, 4 * 128], F32))
        q2s = eD(nc.sbuf_tensor("q2s", [128, 4 * 128], F32))
        sxb = eD(nc.sbuf_tensor("sxb", [128, 132], F32))
        s2bt = eD(nc.sbuf_tensor("s2bt", [128, 128], F32))
        mx2c = eD(nc.sbuf_tensor("mx2c", [128, 1], F32))
        r2c = eD(nc.sbuf_tensor("r2c", [128, 1], F32))
        wsum = eD(nc.sbuf_tensor("wsum", [128, 1], F32))
        mnbuf = eD(nc.sbuf_tensor("mnbuf", [128, 8 * 128], F32))
        wf32b_off = [0, 2048]
        if safe:
            lu3 = scr16
            mxbuf_t, mxbuf_off, mxbuf_pitch = scr16, 0, HC * 128
            ih8 = None
        else:
            ih8 = eD(nc.sbuf_tensor("ih8", [128, HC * 128], I8))
            mxbuf_t, mxbuf_off, mxbuf_pitch = scr16, 0, HC * 128
            lu3 = None

        psW = eD(nc.psum_tensor("psW", [128, 512], F32))
        psM1 = [eD(nc.psum_tensor(f"psM1{k}", [128, 396], F32)) for k in range(3)]
        psM2 = eD(nc.psum_tensor("psM2", [128, 1024], F32))
        psTr = eD(nc.psum_tensor("psTr", [128, 512], BF16))
        psX = eD(nc.psum_tensor("psX", [128, 512], F32))

        wfb_guard = [None, None]
        psW_guard = []
        psM1_guard = [[], [], []]
        psM2_guard = None
        psTr_guard = None
        psX_guard = []
        mnsb_guard = None

        P.emit("gpsimd", None, waits=all_stores())
        if phases < 4:
            P.replay(nc)
            return nc
        P.emit("gpsimd", lambda g: g.dma_start(
            out=cwt[:], in_=bass.AP(cw_d.tensor, 0, [[3, 128], [384, HC], [1, 3]])),
            inc="cwb", inc_n=16)
        P.emit("gpsimd", lambda g: g.dma_start(
            out=cbt[:], in_=bass.AP(cb_d.tensor, 0, [[1, 128], [128, HC], [1, 1]]),
            allow_slow_non_contiguous=True),
            inc="cwb", inc_n=16)
        P.emit("gpsimd", lambda g: g.memset(sxb[:], 0.0), inc="g")
        cw_ready = P.now("cwb")
        sx0_ready = P.now("g")

        def quant_weight(dram, drows, dcols, tcols, dstT, dst_pitch, dst_off_fn,
                         chunk_stride, sscal):
            nonlocal psTr_guard, mnsb_guard, psX_guard, psW_guard
            nrt, nct = drows // 128, dcols // tcols
            tiles = [(r, cp) for r in range(nrt) for cp in range(nct)]
            P.emit("vector", lambda v: v.memset(wsum[:], 0.0), inc="v")
            for ti, (r, cp) in enumerate(tiles):
                wsl = ti % 2
                w = [wfb_guard[wsl]] if wfb_guard[wsl] else []
                P.emit("gpsimd", (lambda r_, cp_, w_: lambda g: g.dma_start(
                    out=scr16[:, wf32b_off[w_]:wf32b_off[w_] + tcols],
                    in_=dram[128 * r_:128 * (r_ + 1), tcols * cp_:tcols * (cp_ + 1)]))(r, cp, wsl),
                    waits=w, inc=f"wf{wsl}", inc_n=16)
                P.emit("vector", (lambda w_: lambda v: v.reduce_sum(
                    out=colA[:], in_=scr16[:, wf32b_off[w_]:wf32b_off[w_] + tcols], axis=AX.X,
                    apply_absolute_value=True))(wsl),
                    waits=[P.now(f"wf{wsl}"), P.now("v")], inc="v")
                P.emit("vector", lambda v: v.tensor_tensor(out=wsum[:], in0=wsum[:], in1=colA[:],
                                                           op=ALU.add),
                       waits=[P.now("v")], inc="v")
                wfb_guard[wsl] = P.now("v")
            tw = [P.now("v")] + psX_guard
            P.emit("tensor", lambda t: t.transpose(psX[0:1, 384:512], wsum[:], identf[:]),
                   waits=tw, inc="t")
            P.emit("vector", lambda v: v.tensor_copy(out=trow[0:1, 0:128], in_=psX[0:1, 384:512]),
                   waits=[P.now("t")], inc="v")
            psX_guard = [P.now("v")]
            P.emit("vector", lambda v: v.reduce_sum(out=srow3[:], in_=trow[0:1, 0:128], axis=AX.X),
                   waits=[P.now("v")], inc="v")
            nel = float(drows * dcols)
            P.emit("vector", (lambda nel_: lambda v: v.tensor_scalar(
                out=srow3[:], in0=srow3[:], scalar1=1.0 / nel_, scalar2=1e-5,
                op0=ALU.mult, op1=ALU.max))(nel),
                waits=[P.now("v")], inc="v")
            P.emit("vector", (lambda ss_: lambda v: v.tensor_copy(out=ss_[:], in_=srow3[:]))(sscal),
                   waits=[P.now("v")], inc="v")
            P.emit("vector", lambda v: v.reciprocal(out=srow3[:], in_=srow3[:]),
                   waits=[P.now("v")], inc="v")
            bw = [P.now("v")] + psW_guard
            P.emit("tensor", lambda t: t.matmul(psW[:, 0:1], onesrow[:], srow3[:],
                                                start=True, stop=True),
                   waits=bw, inc="t")
            P.emit("vector", lambda v: v.tensor_copy(out=colD[:], in_=psW[:, 0:1]),
                   waits=[P.now("t")], inc="v")
            psW_guard = [P.now("v")]
            scol_ready = P.now("v")
            ntr = tcols // 128
            for ti, (r, cp) in enumerate(tiles):
                wsl = ti % 2
                w = [wfb_guard[wsl]] if wfb_guard[wsl] else []
                P.emit("gpsimd", (lambda r_, cp_, w_: lambda g: g.dma_start(
                    out=scr16[:, wf32b_off[w_]:wf32b_off[w_] + tcols],
                    in_=dram[128 * r_:128 * (r_ + 1), tcols * cp_:tcols * (cp_ + 1)]))(r, cp, wsl),
                    waits=w, inc=f"wf{wsl}", inc_n=16)
                ldone = P.now(f"wf{wsl}")
                wfsl = lambda w_: scr16[:, wf32b_off[w_]:wf32b_off[w_] + tcols]
                P.emit("vector", (lambda w_: lambda v: v.tensor_scalar(
                    out=wfsl(w_), in0=wfsl(w_), scalar1=colD[:],
                    scalar2=None, op0=ALU.mult))(wsl),
                    waits=[ldone, scol_ready], inc="v")
                mw = [P.now("v")] + ([mnsb_guard] if mnsb_guard else [])
                P.emit("vector", (lambda w_: lambda v: v.tensor_scalar(
                    out=mnsb[:, 0:tcols], in0=wfsl(w_), scalar1=0.5, scalar2=None,
                    op0=ALU.is_ge))(wsl),
                    waits=mw, inc="v")
                P.emit("vector", (lambda w_: lambda v: v.scalar_tensor_tensor(
                    out=mnsb[:, 0:tcols], in0=wfsl(w_), scalar=-0.5,
                    in1=mnsb[:, 0:tcols], op0=ALU.is_le, op1=ALU.subtract))(wsl),
                    waits=[P.now("v")], inc="v")
                P.emit("vector", lambda v: v.tensor_scalar(
                    out=mnsb[:, 0:tcols], in0=mnsb[:, 0:tcols], scalar1=-1.0,
                    scalar2=None, op0=ALU.mult),
                    waits=[P.now("v")], inc="v")
                q_done = P.now("v")
                wfb_guard[wsl] = q_done
                for tr0 in range(0, ntr, 4):
                    bn = min(4, ntr - tr0)
                    tw = [q_done] + ([psTr_guard] if psTr_guard else [])
                    for q in range(bn):
                        cch = tr0 + q
                        P.emit("tensor", (lambda c_, q_: lambda t: t.transpose(
                            psTr[:, 128 * q_:128 * (q_ + 1)], mnsb[:, 128 * c_:128 * (c_ + 1)],
                            identb[:]))(cch, q),
                            waits=tw if q == 0 else (), inc="t" if q == bn - 1 else None)
                    tdone = P.now("t")
                    def mke3(r_, cp_, tr0_, bn_):
                        def f(v):
                            dst = bass.AP(dstT[:].tensor, dst_off_fn(r_, cp_, tr0_),
                                          [[dst_pitch, 128], [chunk_stride, bn_], [1, 128]])
                            srcp = bass.AP(psTr[:].tensor, 0, [[512, 128], [128, bn_], [1, 128]])
                            return v.tensor_copy(out=dst, in_=srcp)
                        return f
                    P.emit("vector", mke3(r, cp, tr0, bn), waits=[tdone], inc="v")
                    psTr_guard = P.now("v")
                mnsb_guard = P.now("t")
            return P.now("v")

        sw1_ready = quant_weight(w1_d, H, D, D, w1qT, C * H,
                                 lambda r, cp, c: H * c + 128 * r, H, srow2)
        sw2_ready = quant_weight(w2_d, D, H, 2048, w2qT, HC * 1024,
                                 lambda r, cp, c: 1024 * (16 * cp + c) + 128 * r, 1024, srow1)
        bw = [sw2_ready] + psW_guard
        P.emit("tensor", lambda t: t.matmul(psW[:, 0:1], onesrow[:], srow1[:],
                                            start=True, stop=True),
               waits=bw, inc="t")
        P.emit("vector", lambda v: v.tensor_copy(out=w2mc[:], in_=psW[:, 0:1]),
               waits=[P.now("t")], inc="v")
        psW_guard = [P.now("v")]
        w2mc_ready = P.now("v")

        bwc = [sw1_ready] + psW_guard
        P.emit("tensor", lambda t: t.matmul(psW[:, 0:1], onesrow[:], srow2[:],
                                            start=True, stop=True),
               waits=bwc, inc="t")
        P.emit("vector", lambda v: v.tensor_copy(out=colC[:], in_=psW[:, 0:1]),
               waits=[P.now("t")], inc="v")
        psW_guard = [P.now("v")]
        sg1 = -1.0 if safe else 1.0
        P.emit("vector", (lambda sg_: lambda v: v.tensor_scalar(
            out=stat_mx[:], in0=stat_mx[:], scalar1=colC[:], scalar2=sg_ / 127.0,
            op0=ALU.mult, op1=ALU.mult))(sg1),
            waits=[P.now("v")], inc="v")
        tw = [P.now("v")] + psX_guard
        P.emit("tensor", lambda t: t.transpose(psX[0:32, 384:512], stat_mx[:], identf[:]),
               waits=tw, inc="t")
        P.emit("vector", lambda v: v.tensor_copy(out=trow[0:32, 0:128], in_=psX[0:32, 384:512]),
               waits=[P.now("t")], inc="v")
        psX_guard = [P.now("v")]
        def reord(sy):
            srcp = bass.AP(trow[:].tensor, 0, [[128, 32], [1, 128]])
            dst = bass.AP(row1_d.tensor, 0, [[S, 1], [128, 32], [1, 128]])
            return sy.dma_start(out=dst, in_=srcp)
        P.emit("sync", reord, waits=[P.now("v")], inc="ror", inc_n=16)
        sxrow_ready = P.now("ror")

        ixw_guard = [None, None]
        xres_guard = [None, None]
        ares_guard = [None, None]
        sxb_guard = None
        last_conv_v = None

        def ixload(tau):
            sl = tau % 2
            w = [ixw_guard[sl]] if ixw_guard[sl] else []
            P.emit("gpsimd", (lambda t_, sl_: lambda g: g.dma_start(
                out=bass.AP(ixw[sl_][:].tensor, 0, [[C * 132, 128], [132, C], [1, 130]]),
                in_=bass.AP(ixT_d.tensor, 128 * t_, [[S + 2, 128], [128 * (S + 2), C], [1, 130]])))(tau, sl),
                waits=w, inc=f"iw{sl}", inc_n=16)
            return P.now(f"iw{sl}")

        def xaload(tau):
            sl = tau % 2
            w = [xres_guard[sl]] if xres_guard[sl] else []
            P.emit("gpsimd", (lambda t_, sl_: lambda g: g.dma_start(
                out=xres[sl_][:], in_=x_d[128 * t_:128 * (t_ + 1), :]))(tau, sl),
                waits=w, inc=f"xr{sl}", inc_n=16)
            xr_pt = P.now(f"xr{sl}")
            w2_ = [ares_guard[sl]] if ares_guard[sl] else []
            P.emit("gpsimd", (lambda t_, sl_: lambda g: g.dma_start(
                out=ares[sl_][:], in_=attn_d[128 * t_:128 * (t_ + 1), :]))(tau, sl),
                waits=w2_, inc=f"ar{sl}", inc_n=16)
            return [xr_pt, P.now(f"ar{sl}")]

        ixready = [None] * T
        xaready = [None] * T
        ixready[0] = ixload(0)
        xaready[0] = xaload(0)

        for tau in range(T):
            sl = tau % 2
            if tau + 1 < T:
                ixready[tau + 1] = ixload(tau + 1)
                xaready[tau + 1] = xaload(tau + 1)

            lo = max(0, 128 * tau - 1)
            hi = min(S, 128 * tau + 129)
            wcols = hi - lo
            off = 1 if tau == 0 else 0
            sxw = [sxrow_ready, sx0_ready] + ([sxb_guard] if sxb_guard else [])
            P.emit("gpsimd", (lambda lo_, off_, wc_: lambda g: g.dma_start(
                out=sxb[:, off_:off_ + wc_],
                in_=bass.AP(row1_d.tensor, lo_, [[0, 128], [1, wc_]])))(lo, off, wcols),
                waits=sxw, inc="sx", inc_n=16)
            sxb_ready = P.now("sx")

            ngrp = (HC + 2) // 3
            h2_ready = []
            for grp in range(ngrp):
                c0 = 3 * grp
                cn = min(3, HC - c0)
                pslot = grp % 3
                ps = psM1[pslot]
                mmw = [ixready[tau], sw1_ready] + psM1_guard[pslot]
                for q in range(cn):
                    hc = c0 + q
                    for din in range(C):
                        P.emit("tensor", (lambda hc_, q_, di_, ps_, sl_: lambda t: t.matmul(
                            ps_[:, 132 * q_:132 * q_ + 130],
                            w1qT[:, H * di_ + 128 * hc_:H * di_ + 128 * (hc_ + 1)],
                            ixw[sl_][:, 132 * di_:132 * di_ + 130],
                            start=(di_ == 0), stop=(di_ == C - 1)))(hc, q, din, ps, sl),
                            waits=mmw if (q == 0 and din == 0) else (),
                            inc="t" if (q == cn - 1 and din == C - 1) else None)
                mmdone = P.now("t")
                def mkdq(ps_, c0_, cn_):
                    def f(v):
                        srcp = bass.AP(ps_[:].tensor, 0, [[396, 128], [132, cn_], [1, 130]])
                        dst = bass.AP(h2[:].tensor, 132 * c0_, [[HC * 132, 128], [132, cn_], [1, 130]])
                        rep = bass.AP(sxb[:].tensor, 0, [[132, 128], [0, cn_], [1, 130]])
                        return v.tensor_tensor(out=dst, in0=srcp, in1=rep, op=ALU.mult)
                    return f
                P.emit("vector", mkdq(ps, c0, cn), waits=[mmdone, sxb_ready], inc="v")
                psM1_guard[pslot] = [P.now("v")]
                h2_ready.append(P.now("v"))
            ixw_guard[sl] = P.now("t")
            sxb_guard = h2_ready[-1]

            for b0 in range(0, HC, 4):
                need = h2_ready[min((b0 + 3) // 3, ngrp - 1)]
                sw0 = [need, cw_ready] + ([last_conv_v] if last_conv_v else [])
                for q in range(4):
                    hc = b0 + q
                    P.emit("scalar", (lambda hc_, q_: lambda s: s.activation(
                        out=q0s[:, 128 * q_:128 * (q_ + 1)], in_=h2[:, 132 * hc_:132 * hc_ + 128],
                        func=ACT.Copy, bias=0.0, scale=cwt[:, 3 * hc_:3 * hc_ + 1]))(hc, q),
                        waits=sw0 if q == 0 else (), inc="s" if q == 3 else None)
                q0done = P.now("s")
                for q in range(4):
                    hc = b0 + q
                    P.emit("scalar", (lambda hc_, q_: lambda s: s.activation(
                        out=q2s[:, 128 * q_:128 * (q_ + 1)], in_=h2[:, 132 * hc_ + 2:132 * hc_ + 130],
                        func=ACT.Copy, bias=0.0,
                        scale=cwt[:, 3 * hc_ + 2:3 * hc_ + 3]))(hc, q),
                        inc="s" if q == 3 else None)
                q2done = P.now("s")
                for q in range(4):
                    hc = b0 + q
                    P.emit("vector", (lambda hc_, q_: lambda v: v.scalar_tensor_tensor(
                        out=q0s[:, 128 * q_:128 * (q_ + 1)], in0=h2[:, 132 * hc_ + 1:132 * hc_ + 129],
                        scalar=cwt[:, 3 * hc_ + 1:3 * hc_ + 2], in1=q0s[:, 128 * q_:128 * (q_ + 1)],
                        op0=ALU.mult, op1=ALU.add))(hc, q),
                        waits=[q0done] if q == 0 else (), inc="v" if q == 3 else None)
                ydone = P.now("v")
                for q in range(4):
                    hc = b0 + q
                    P.emit("vector", (lambda hc_, q_: lambda v: v.scalar_tensor_tensor(
                        out=h2[:, 132 * hc_ + 1:132 * hc_ + 129], in0=q0s[:, 128 * q_:128 * (q_ + 1)],
                        scalar=cbt[:, hc_:hc_ + 1], in1=q2s[:, 128 * q_:128 * (q_ + 1)],
                        op0=ALU.add, op1=ALU.add))(hc, q),
                        waits=[ydone, q2done] if q == 0 else (), inc="v" if q == 3 else None)
                last_conv_v = P.now("v")
            h2slices = bass.AP(h2[:].tensor, 1, [[HC * 132, 128], [132, HC], [1, 128]])
            P.emit("scalar", (lambda ap_: lambda s: s.activation(
                out=ap_, in_=ap_, func=ACT.Silu, bias=0.0, scale=1.0))(h2slices),
                waits=[last_conv_v], inc="s")
            silu_done = P.now("s")

            def fold(opk, buf_t, buf_off, buf_pitch, start_wait):
                def l1(k):
                    def f(v):
                        i0 = (bass.AP(buf_t[:].tensor, buf_off, [[buf_pitch, 128], [128, 8], [1, 128]])
                              if k > 0 else
                              bass.AP(h2[:].tensor, 1, [[HC * 132, 128], [132, 8], [1, 128]]))
                        i1 = bass.AP(h2[:].tensor, 1 + 8 * (k + 1) * 132,
                                     [[HC * 132, 128], [132, 8], [1, 128]])
                        o = bass.AP(buf_t[:].tensor, buf_off, [[buf_pitch, 128], [128, 8], [1, 128]])
                        return v.tensor_tensor(out=o, in0=i0, in1=i1, op=opk)
                    return f
                P.emit("vector", l1(0), waits=start_wait, inc="v")
                P.emit("vector", l1(1), waits=[P.now("v")], inc="v")
                P.emit("vector", l1(2), waits=[P.now("v")], inc="v")
                n = 4
                while n >= 1:
                    def lv(n_):
                        def f(v):
                            i0 = bass.AP(buf_t[:].tensor, buf_off, [[buf_pitch, 128], [128, n_], [1, 128]])
                            i1 = bass.AP(buf_t[:].tensor, buf_off + n_ * 128,
                                         [[buf_pitch, 128], [128, n_], [1, 128]])
                            o = bass.AP(buf_t[:].tensor, buf_off, [[buf_pitch, 128], [128, n_], [1, 128]])
                            return v.tensor_tensor(out=o, in0=i0, in1=i1, op=opk)
                        return f
                    P.emit("vector", lv(n), waits=[P.now("v")], inc="v")
                    n //= 2
                return P.now("v")
            fold(ALU.max, mxbuf_t, mxbuf_off, mxbuf_pitch, [silu_done])
            mn_done = fold(ALU.min, mnbuf, 0, 8 * 128, [P.now("v")])
            P.emit("vector", lambda v: v.tensor_scalar(out=mnbuf[:, 0:128], in0=mnbuf[:, 0:128],
                                                       scalar1=-1.0, scalar2=None, op0=ALU.mult),
                   waits=[mn_done], inc="v")
            P.emit("vector", (lambda bt_, bo_, bp_: lambda v: v.tensor_tensor(
                out=bass.AP(bt_[:].tensor, bo_, [[bp_, 128], [1, 128]]),
                in0=bass.AP(bt_[:].tensor, bo_, [[bp_, 128], [1, 128]]),
                in1=mnbuf[:, 0:128], op=ALU.max))(mxbuf_t, mxbuf_off, mxbuf_pitch),
                waits=[P.now("v")], inc="v")
            cmb_done = P.now("v")
            tw2 = [cmb_done] + psX_guard
            P.emit("tensor", (lambda bt_, bo_, bp_: lambda t: t.transpose(
                psX[:, 384:512],
                bass.AP(bt_[:].tensor, bo_, [[bp_, 128], [1, 128]]), identf[:]))(
                mxbuf_t, mxbuf_off, mxbuf_pitch),
                waits=tw2, inc="t")
            P.emit("vector", lambda v: v.reduce_max(out=mx2c[:], in_=psX[:, 384:512], axis=AX.X),
                   waits=[P.now("t")], inc="v")
            psX_guard = [P.now("v")]
            P.emit("vector", lambda v: v.tensor_scalar(out=mx2c[:], in0=mx2c[:], scalar1=1e-5,
                                                       scalar2=None, op0=ALU.max),
                   waits=[P.now("v")], inc="v")
            P.emit("vector", lambda v: v.reciprocal(out=r2c[:], in_=mx2c[:]),
                   waits=[P.now("v")], inc="v")
            P.emit("vector", lambda v: v.tensor_scalar(out=r2c[:], in0=r2c[:], scalar1=127.0,
                                                       scalar2=None, op0=ALU.mult),
                   waits=[P.now("v")], inc="v")
            s2_ready = P.now("v")
            tw3 = [s2_ready] + psX_guard
            P.emit("tensor", lambda t: t.transpose(psX[0:1, 384:512], r2c[:], identf[:]),
                   waits=tw3, inc="t")
            P.emit("vector", lambda v: v.tensor_copy(out=trow[0:1, 0:128], in_=psX[0:1, 384:512]),
                   waits=[P.now("t")], inc="v")
            psX_guard = [P.now("v")]
            bw3 = [P.now("v")] + psX_guard
            P.emit("tensor", lambda t: t.matmul(psX[:, 0:128], onesrow[:], trow[0:1, 0:128],
                                                start=True, stop=True),
                   waits=bw3, inc="t")
            P.emit("vector", lambda v: v.tensor_copy(out=s2bt[:], in_=psX[:, 0:128]),
                   waits=[P.now("t")], inc="v")
            psX_guard = [P.now("v")]
            s2b_ready = P.now("v")

            h2c = bass.AP(h2[:].tensor, 1, [[HC * 132, 128], [132, HC], [1, 128]])
            s2rep = bass.AP(s2bt[:].tensor, 0, [[128, 128], [0, HC], [1, 128]])
            P.emit("vector", lambda v: v.tensor_tensor(out=h2c, in0=h2c, in1=s2rep, op=ALU.mult),
                   waits=[s2b_ready], inc="v")
            p_done = P.now("v")
            if not safe:
                P.emit("vector", lambda v: v.tensor_copy(
                    out=bass.AP(ih8[:].tensor, 0, [[HC * 128, 128], [128, HC], [1, 128]]),
                    in_=h2c), waits=[p_done], inc="v")
                P.emit("vector", lambda v: v.tensor_copy(out=ihb[:], in_=ih8[:]),
                       waits=[P.now("v")], inc="v")
            else:
                P.emit("scalar", lambda s: s.activation(out=mnsb[:], in_=h2c, func=ACT.Sign,
                                                        bias=halfcol[:], scale=1.0),
                       waits=[p_done], inc="s")
                for kk, cc in enumerate(LADDER):
                    if kk == 0:
                        P.emit("vector", (lambda cc_: lambda v: v.scalar_tensor_tensor(
                            out=lu3[:], in0=mnsb[:], scalar=-cc_, in1=h2c,
                            op0=ALU.mult, op1=ALU.add))(cc),
                            waits=[P.now("s")], inc="v")
                    else:
                        P.emit("vector", (lambda cc_: lambda v: v.scalar_tensor_tensor(
                            out=lu3[:], in0=mnsb[:], scalar=-cc_, in1=lu3[:],
                            op0=ALU.mult, op1=ALU.add))(cc),
                            waits=[P.now("s")], inc="v")
                    P.emit("scalar", (lambda last: lambda s: s.activation(
                        out=mnsb[:], in_=lu3[:], func=ACT.Sign,
                        bias=(0.0 if last else halfcol[:]), scale=1.0))(kk == 7),
                        waits=[P.now("v")], inc="s")
                P.emit("vector", lambda v: v.scalar_tensor_tensor(
                    out=lu3[:], in0=mnsb[:], scalar=-0.5, in1=lu3[:], op0=ALU.mult, op1=ALU.add),
                    waits=[P.now("s")], inc="v")
                P.emit("vector", lambda v: v.tensor_tensor(
                    out=bass.AP(ihb[:].tensor, 0, [[HC * 128, 128], [128, HC], [1, 128]]),
                    in0=bass.AP(lu3[:].tensor, 0, [[HC * 128, 128], [128, HC], [1, 128]]),
                    in1=h2c, op=ALU.subtract),
                    waits=[P.now("v")], inc="v")
            ih_ready = P.now("v")

            mmw = [ih_ready, sw2_ready] + ([psM2_guard] if psM2_guard else [])
            for hc in range(HC):
                for hf in range(2):
                    P.emit("tensor", (lambda hc_, hf_: lambda t: t.matmul(
                        psM2[:, 512 * hf_:512 * (hf_ + 1)],
                        ihb[:, 128 * hc_:128 * (hc_ + 1)],
                        w2qT[:, 1024 * hc_ + 512 * hf_:1024 * hc_ + 512 * (hf_ + 1)],
                        start=(hc_ == 0), stop=(hc_ == HC - 1)))(hc, hf),
                        waits=mmw if (hc == 0 and hf == 0) else (),
                        inc="t" if (hc == HC - 1 and hf == 1) else None)
            m2done = P.now("t")
            sg2 = -1.0 if safe else 1.0
            P.emit("vector", (lambda sg_: lambda v: v.tensor_scalar(
                out=r2c[:], in0=mx2c[:], scalar1=w2mc[:], scalar2=sg_ / 127.0,
                op0=ALU.mult, op1=ALU.mult))(sg2),
                waits=[w2mc_ready], inc="v")
            aw = list(xaready[tau]) + [P.now("v")]
            P.emit("vector", (lambda sl_: lambda v: v.tensor_tensor(
                out=xres[sl_][:], in0=xres[sl_][:], in1=ares[sl_][:], op=ALU.add))(sl),
                waits=aw, inc="v")
            ares_guard[sl] = P.now("v")
            P.emit("vector", (lambda sl_: lambda v: v.scalar_tensor_tensor(
                out=xres[sl_][:], in0=psM2[:], scalar=r2c[:], in1=xres[sl_][:],
                op0=ALU.mult, op1=ALU.add))(sl),
                waits=[P.now("v"), m2done], inc="v")
            psM2_guard = P.now("v")
            P.emit("sync", (lambda tau_, sl_: lambda sy: sy.dma_start(
                out=out_d[128 * tau_:128 * (tau_ + 1), :], in_=xres[sl_][:]))(tau, sl),
                waits=[P.now("v")], inc=f"ou{sl}", inc_n=16)
            xres_guard[sl] = P.now(f"ou{sl}")

        P.emit("sync", None, waits=all_stores())
        P.replay(nc)

    st.close()
    return nc


def kernel(L0_attn_embeddings, L0_attn_W_Q, L0_attn_W_K, L0_attn_W_V,
           L0_ffn_w1, L0_ffn_conv_w, L0_ffn_conv_b, L0_ffn_w2):
    x = np.ascontiguousarray(np.asarray(L0_attn_embeddings, dtype=np.float32))
    wq = np.ascontiguousarray(np.asarray(L0_attn_W_Q, dtype=np.float32))
    wk = np.ascontiguousarray(np.asarray(L0_attn_W_K, dtype=np.float32))
    wv = np.ascontiguousarray(np.asarray(L0_attn_W_V, dtype=np.float32))
    w1 = np.ascontiguousarray(np.asarray(L0_ffn_w1, dtype=np.float32))
    cw = np.ascontiguousarray(np.asarray(L0_ffn_conv_w, dtype=np.float32).reshape(H, 3))
    cb = np.ascontiguousarray(np.asarray(L0_ffn_conv_b, dtype=np.float32))
    w2 = np.ascontiguousarray(np.asarray(L0_ffn_w2, dtype=np.float32))

    nc = build_nc(SAFE_ROUND)
    in_maps = [dict(x=np.ascontiguousarray(x[i]), wq=wq, wk=wk, wv=wv,
                    w1=w1, cw=cw, cb=cb, w2=w2) for i in range(8)]
    res = run_bass_kernel_spmd(nc, in_maps, core_ids=list(range(8)))
    return np.stack([np.asarray(res.results[i]["out"], dtype=np.float32) for i in range(8)])
